# version 04 (final)
# baseline (speedup 1.0000x reference)
import numpy as np
import ml_dtypes
import concourse.bass as bass
import concourse.mybir as mybir
from concourse.bass_utils import run_bass_kernel_spmd
from contextlib import ExitStack

F32 = mybir.dt.float32
BF16 = mybir.dt.bfloat16
I32 = mybir.dt.int32
AF = mybir.ActivationFunctionType
ALU = mybir.AluOpType
AX = mybir.AxisListType.X
ts = bass.ts

B, SEQ, D = 4, 2048, 4096
HALF = SEQ // 2
NHC = 8
HD = 256
KT = D // 128
NTS = SEQ // 128
NTH = HALF // 128
RG = [[0, 1], [2, 3], [4, 5], [6, 7]]



def _ga_tables():
    g = np.array([1.0, 1.0, 1.0], dtype=np.float32)
    dim = len(g); n = 2 ** dim
    masks = sorted(range(n), key=lambda m: (bin(m).count('1'), m))
    inv = {m: i for i, m in enumerate(masks)}
    C = np.zeros((n, n, n), dtype=np.float32)
    for i, a in enumerate(masks):
        for j, bb in enumerate(masks):
            sw, at = (0, a >> 1)
            while at:
                sw += bin(at & bb).count('1'); at >>= 1
            s = -1.0 if sw & 1 else 1.0
            for t in range(dim):
                if a >> t & 1 and bb >> t & 1:
                    s *= float(g[t])
            C[i, j, inv[a ^ bb]] = s
    grades = np.array([bin(m).count('1') for m in masks])
    rev = np.where(grades * (grades - 1) // 2 % 2 == 1, -1.0, 1.0).astype(np.float32)
    met = np.array([float(np.prod([g[t] for t in range(dim) if m >> t & 1] + [1.0]))
                    for m in masks], dtype=np.float32)
    return C[:, :, [4, 5, 6]], rev, met


def _t_basis():
    biv, rev, met = _ga_tables()
    e0 = np.diag(rev * met * (1.0 / 16.0)).astype(np.float32)
    basis = np.zeros((4, 128, 128), np.float32)
    for dd in range(16):
        sl = slice(dd * 8, (dd + 1) * 8)
        basis[0][sl, sl] = e0
        for c in range(3):
            basis[c + 1][sl, sl] = rev[:, None] * biv[:, :, c]
    return basis


def _sincos():
    inv_freq = 1.0 / 10000.0 ** (np.arange(0, 64, 2, dtype=np.float32) / 64)
    ang = np.arange(SEQ, dtype=np.float32)[:, None] * inv_freq[None, :]
    def tm(t):
        return np.ascontiguousarray(t.reshape(NTS, 128, 32).transpose(1, 0, 2))
    return tm(np.cos(ang).astype(np.float32)), tm(np.sin(ang).astype(np.float32))


def _perm_tileA(h):
    base = h * HD
    return ([base + 2 * t for t in range(32)] + [base + 2 * t + 1 for t in range(32)]
            + list(range(base + 64, base + 128)))


def _rhs_tiles(wt, couts, csz):
    w = np.ascontiguousarray(wt[np.array(couts)].T)
    nch = w.shape[1] // csz
    return np.ascontiguousarray(w.reshape(KT, 128, nch, csz).transpose(2, 1, 0, 3))


def _lhsT_tiles(wt, couts):
    w = np.ascontiguousarray(wt[np.array(couts)].T)
    nct = w.shape[1] // 128
    a = w.reshape(KT, 128, nct, 128)
    return np.ascontiguousarray(a.transpose(2, 1, 0, 3))


ENGS = ("tensor", "vector", "scalar", "sync", "gpsimd")


class Prog:
    def __init__(self, sems):
        self.ops = {e: [] for e in ENGS}
        self.cnt = {k: 0 for k in sems}
        self.sems = sems
        self.frozen = False

    def emit(self, eng, fn):
        if self.frozen:
            return
        self.ops[eng].append(fn)

    def wait(self, eng, sem, val):
        if val is None or val <= 0:
            return
        s = self.sems[sem]
        self.emit(eng, lambda E, s=s, v=val: E.wait_ge(s, v))

    def waitp(self, eng, pair):
        if pair is not None:
            self.wait(eng, pair[0], pair[1])

    def inc(self, sem, amt=1):
        if self.frozen:
            return self.cnt[sem]
        self.cnt[sem] += amt
        return self.cnt[sem]

    def op(self, eng, meth, *args, inc=None, amt=1, **kw):
        if inc is None:
            self.emit(eng, lambda E, m=meth, a=args, k=kw: getattr(E, m)(*a, **k))
            return None
        tgt = self.inc(inc, amt)
        s = self.sems[inc]
        if eng in ("vector", "scalar") and meth != "drain":
            self.emit(eng, lambda E, m=meth, a=args, k=kw: getattr(E, m)(*a, **k))
            self.emit(eng, lambda E, s=s, am=amt: E.drain().then_inc(s, am))
        else:
            self.emit(eng, lambda E, m=meth, a=args, k=kw, s=s, am=amt:
                      getattr(E, m)(*a, **k).then_inc(s, am))
        return tgt

    def barrier(self):
        self.wait("sync", "syd", self.cnt["syd"])
        self.wait("gpsimd", "gpd", self.cnt["gpd"])
        mine = {"tensor": "t", "vector": "v", "scalar": "s", "sync": "bsy", "gpsimd": "bgp"}
        vals = {e: self.op(e, "drain" if e in ("tensor", "vector", "scalar") else "nop",
                           inc=sem) for e, sem in mine.items()}
        for e in ENGS:
            for o, sem in mine.items():
                if o != e:
                    self.wait(e, sem, vals[o])


def build(upto="full"):
    nc = bass.Bass("TRN2", target_bir_lowering=False)
    d = {}
    def din(name, shape, dt=F32):
        d[name] = nc.dram_tensor(name, list(shape), dt, kind="ExternalInput").ap()
    din("x0", (SEQ, D))
    din("x0h", (HALF, D))
    din("wqa", (4, 128, KT, 256)); din("wka", (4, 128, KT, 256))
    din("wqb", (NHC, 128, KT, 128)); din("wkb", (NHC, 128, KT, 128))
    din("wv", (8, 128, KT, 256))
    din("wo", (16, 128, KT, 256))
    din("wf", (16, 128, KT, 256))
    din("fbias", (1, D))
    din("lA", (1, 128, KT, 48))
    din("lB", (48, D))
    din("lsi", (1, 4), I32)
    din("myb", (1, 1))
    din("offs", (1, 4), I32)
    din("wbq", (16, 128, KT, 128)); din("wbk", (16, 128, KT, 128))
    din("wbv", (8, 128, KT, 256))
    din("bqc", (128, 16)); din("bkc", (128, 16))
    din("bvrow", (1, 2048))
    din("bw", (1, 3 * NHC))
    din("identc", (128, 128), BF16)
    din("ltmask", (128, 128), BF16)
    din("cosc", (128, NTS, 32)); din("sinc", (128, NTS, 32))
    din("tbasis", (4, 128, 128))
    din("onesc", (1, 512), BF16)
    din("i3", (1, 3))
    out = nc.dram_tensor("out", [HALF, D], F32, kind="ExternalOutput").ap()

    dbgk = {"qkv": ("qt_s", "kt_s", "vs_s"), "bqkv": ("qt_s", "kt_s", "vs_s"),
            "qa": ("qt_s",), "qa1": ("qt_s",), "qa2": ("qt_s",), "qa3": ("qt_s",),
            "qb": ("qt_s",), "v0": ("vs_s",),
            "ao": ("cc1i",), "bo": ("cc3i",)}.get(upto, ())
    def _k(nm):
        return "ExternalOutput" if nm in dbgk else "Internal"
    if upto == "ln":
        dbgT = nc.dram_tensor("dbg_ln", [128, 32768], BF16, kind="ExternalOutput").ap()
        dbgA5 = nc.dram_tensor("dbg_a5", [128, 2048], F32, kind="ExternalOutput").ap()
        dbgA6 = nc.dram_tensor("dbg_a6", [128, 4096], BF16, kind="ExternalOutput").ap()

    qt = nc.dram_tensor("qt_s", [16, 128, SEQ], BF16, kind=_k("qt_s")).ap()
    kts = nc.dram_tensor("kt_s", [16, 128, SEQ], BF16, kind=_k("kt_s")).ap()
    vs = nc.dram_tensor("vs_s", [SEQ, 2048], BF16, kind=_k("vs_s")).ap()
    x1s = nc.dram_tensor("x1_s", [HALF, D], F32).ap()
    cc1i = nc.dram_tensor("cc1i", [2048, SEQ], BF16, kind=_k("cc1i"))
    cc1o = nc.dram_tensor("cc1o", [4096, SEQ], BF16)
    cc2i = nc.dram_tensor("cc2i", [HALF, D], F32)
    cc2o = nc.dram_tensor("cc2o", [SEQ, D], F32)
    cc3i = nc.dram_tensor("cc3i", [SEQ, 2048], F32, kind=_k("cc3i"))
    cc3o = nc.dram_tensor("cc3o", [2 * SEQ, 2048], F32)
    x3a = nc.dram_tensor("x3a_s", [HALF, 2048], F32).ap()
    x3b = nc.dram_tensor("x3b_s", [HALF, 2048], F32).ap()

    with ExitStack() as es:
        def sbuf(name, shape, dt):
            return es.enter_context(nc.sbuf_tensor(name, list(shape), dt))
        ar1 = sbuf("ar1", [128, 32768], BF16)
        ar2 = sbuf("ar2", [128, 16384], BF16)
        ar3 = sbuf("ar3", [128, 8192], F32)
        ar5 = sbuf("ar5", [128, 3072], F32)
        ar6 = sbuf("ar6", [128, 4096], BF16)
        cosb = sbuf("cosb", [128, NTS * 32], F32)
        sinb = sbuf("sinb", [128, NTS * 32], F32)
        idn = sbuf("idn", [128, 128], BF16)
        ltm = sbuf("ltm", [128, 128], BF16)
        tbas = sbuf("tbas", [128, 4 * 128], F32)
        tblk = sbuf("tblk", [128, NHC * 128], BF16)
        onesb = sbuf("onesb", [1, 512], BF16)
        brow = sbuf("brow", [1, 4096], BF16)
        bvr = sbuf("bvr", [1, 2048], BF16)
        bqcs = sbuf("bqcs", [128, 16], F32)
        bkcs = sbuf("bkcs", [128, 16], F32)
        bwr = sbuf("bwr", [1, 3 * NHC], F32)
        bwb = sbuf("bwb", [128, 3 * NHC], F32)
        lsii = sbuf("lsii", [1, 4], I32)
        offsi = sbuf("offsi", [1, 4], I32)
        maskw = sbuf("maskw", [128, 48], F32)
        lAb = sbuf("lAb", [128, KT * 48], BF16)
        lBb = sbuf("lBb", [48, 4096], BF16)
        tlora = sbuf("tlora", [128, 2 * 48], BF16)
        tloraT = sbuf("tloraT", [48, NTH * 128], BF16)

        ps32 = es.enter_context(nc.psum_tensor("ps32", [128, 6 * 512], F32))
        pstr = es.enter_context(nc.psum_tensor("pstr", [128, 2 * 1024], BF16))

        semnames = ["t", "v", "s", "syd", "gpd", "cc", "bsy", "bgp"]
        sems = {n: es.enter_context(nc.semaphore(n)) for n in semnames}
        block = es.enter_context(nc.Block())

        P = Prog(sems)
        Wt = P.wait
        rt = {}

        lsif = ar5[:, 1088:1092]
        mybf = ar5[:, 1092:1093]
        cvec = ar5[:, 1096:1100]
        segv = ar5[:, 1100:1101]
        usev = ar5[:, 1101:1102]
        idxv = ar5[:, 1102:1103]
        tmp1 = ar5[:, 1103:1104]
        tmp2 = ar5[:, 1104:1105]
        w3 = ar5[:, 1108:1111]
        i3r = ar5[:, 1111:1114]
        w48 = ar5[:, 1120:1168]
        ones128 = ar5[:, 0:128]
        tfb = ar5[:, 352:480]
        stats_off = [768, 800]

        def dma(queue, outap, inap, **kw):
            sem = "syd" if queue == "sync" else "gpd"
            return P.op(queue, "dma_start", inc=sem, amt=16, out=outap, in_=inap, **kw)

        def dma_dyn(queue, fn):
            sem = "syd" if queue == "sync" else "gpd"
            tgt = P.inc(sem, 16)
            s = sems[sem]
            P.emit(queue, lambda E, f=fn, s=s: f(E, rt).then_inc(s, 16))
            return tgt

        state = {"pst_rel": [None, None], "ps_rel": [None] * 6, "w_rel": [None, None],
                 "xt_rel": [None, None], "xl_rel": None,
                 "rot_rel": [None, None], "stg_rel": [None, None],
                 "res_rel": [None, None], "ost_rel": [None, None],
                 "lg_rel": [None, None], "tl_rel": None,
                 "pbuf_rel": None, "ptb_rel": None, "qkv_rel": None,
                 "aoh_rel": None, "rbig_rel": None, "lastdma": 0}

        def psb(i):
            return ps32[:, i * 512:(i + 1) * 512]

        cl = [dma("sync", idn[:, :], d["identc"]),
              dma("sync", ltm[:, :], d["ltmask"]),
              dma("sync", cosb[:, :], d["cosc"].rearrange("p a b -> p (a b)")),
              dma("sync", sinb[:, :], d["sinc"].rearrange("p a b -> p (a b)")),
              dma("sync", tbas[:, :].rearrange("p (a b) -> p a b", a=4),
                  d["tbasis"].transpose([1, 0, 2])),
              dma("sync", bqcs[:, :], d["bqc"]),
              dma("sync", bkcs[:, :], d["bkc"]),
              dma("sync", bwr[:, :], d["bw"]),
              dma("sync", lsii[:, :], d["lsi"]),
              dma("sync", offsi[:, :], d["offs"]),
              dma("sync", mybf[0:1, :], d["myb"]),
              dma("sync", i3r[0:1, :], d["i3"]),
              dma("sync", onesb[:, :], d["onesc"])]
        ldc = max(cl)
        gl = [dma("gpsimd", lBb[:, :], d["lB"]),
              dma("gpsimd", lAb[:, :], d["lA"].rearrange("a p k c -> p (a k c)")),
              dma("gpsimd", brow[:, :], d["fbias"]),
              dma("gpsimd", bvr[:, :], d["bvrow"])]
        gld = max(gl)

        Wt("sync", "syd", ldc)
        def _load_rt(E):
            rt["v"] = E.value_load(offsi[0:1, 1:2], min_val=0, max_val=1)
            return E.nop()
        P.emit("sync", _load_rt)

        Wt("vector", "syd", ldc)
        Wt("vector", "gpd", gld)
        def vop(meth, *a, **k):
            P.op("vector", meth, *a, **k)
            P.op("vector", "drain")
        vop("memset", ones128[0:1, :], 1.0)
        vop("tensor_copy", lsif[0:1, :], lsii[0:1, :])
        vop("tensor_scalar", cvec[0:1, :], lsif[0:1, :], mybf[0:1, :], -1.0,
            ALU.subtract, ALU.mult)
        vop("tensor_scalar", cvec[0:1, :], cvec[0:1, :], 1.0, 0.0, ALU.add, ALU.max)
        vop("tensor_scalar_min", cvec[0:1, :], cvec[0:1, :], 1.0)
        vop("tensor_scalar_add", tmp1[0:1, :], cvec[0:1, 3:4], 3.0)
        vop("tensor_tensor", tmp1[0:1, :], tmp1[0:1, :], cvec[0:1, 2:3], ALU.mult)
        vop("tensor_scalar", tmp2[0:1, :], cvec[0:1, 1:2], -1.0, 1.0, ALU.mult, ALU.add)
        vop("tensor_tensor", tmp2[0:1, :], tmp2[0:1, :], cvec[0:1, 0:1], ALU.mult)
        vop("scalar_tensor_tensor", tmp2[0:1, :], cvec[0:1, 1:2], 2.0, tmp2[0:1, :],
            ALU.mult, ALU.add)
        vop("tensor_scalar", segv[0:1, :], cvec[0:1, 2:3], -1.0, 1.0, ALU.mult, ALU.add)
        vop("tensor_tensor", segv[0:1, :], segv[0:1, :], tmp2[0:1, :], ALU.mult)
        vop("tensor_tensor", segv[0:1, :], segv[0:1, :], tmp1[0:1, :], ALU.add)
        vop("tensor_scalar_add", segv[0:1, :], segv[0:1, :], -1.0)
        vop("tensor_scalar", usev[0:1, :], segv[0:1, :], 0.0, 1.0, ALU.max, ALU.min)
        vop("tensor_scalar", idxv[0:1, :], segv[0:1, :], -1.0, 0.0, ALU.add, ALU.max)
        vop("tensor_scalar_min", idxv[0:1, :], idxv[0:1, :], 2.0)
        vop("tensor_scalar", w3[0:1, :], i3r[0:1, :], idxv[0:1, :], usev[0:1, :],
            ALU.is_equal, ALU.mult)
        P.op("vector", "tensor_copy", w48[0:1, :].rearrange("p (a c) -> p a c", c=16),
             w3[0:1, :].unsqueeze(2).broadcast_to([1, 3, 16]), inc="v")
        v_rows = P.cnt["v"]
        Wt("tensor", "v", v_rows)
        P.op("tensor", "matmul", ps32[:, 0:3 * NHC], ones128[0:1, :], bwr[0:1, :],
             start=True, stop=True)
        P.op("tensor", "matmul", ps32[:, 512:560], ones128[0:1, :], w48[0:1, :],
             start=True, stop=True, inc="t")
        t_bc = P.cnt["t"]
        Wt("scalar", "t", t_bc)
        P.op("scalar", "copy", bwb[:, :], ps32[:, 0:3 * NHC])
        P.op("scalar", "copy", maskw[:, :], ps32[:, 512:560], inc="s")
        s_bm = P.cnt["s"]
        Wt("vector", "s", s_bm)
        for h in range(NHC):
            tdst = tblk[:, h * 128:(h + 1) * 128]
            vop("scalar_tensor_tensor", tfb, tbas[:, 128:256],
                bwb[:, 3 * h:3 * h + 1], tbas[:, 0:128], ALU.mult, ALU.add)
            vop("scalar_tensor_tensor", tfb, tbas[:, 256:384],
                bwb[:, 3 * h + 1:3 * h + 2], tfb, ALU.mult, ALU.add)
            vop("scalar_tensor_tensor", tdst, tbas[:, 384:512],
                bwb[:, 3 * h + 2:3 * h + 3], tfb, ALU.mult, ALU.add)
        P.op("vector", "nop", inc="v")
        P.barrier()

        def xlnT(k, t0, n):
            return ar1[:, :].rearrange("p (k s) -> p k s", k=KT)[:, k:k + 1, t0:t0 + n].squeeze(1)

        def emit_ln(loaders, ntiles):
            for ti in range(ntiles):
                slot = ti % 2
                so = stats_off[slot]
                sums8 = ar5[:, so:so + 8]; ssq = ar5[:, so + 8:so + 9]
                mvec = ar5[:, so + 9:so + 10]; nmsq = ar5[:, so + 10:so + 11]
                varv = ar5[:, so + 11:so + 12]; rcp = ar5[:, so + 12:so + 13]
                rsv = ar5[:, so + 13:so + 14]; bbv = ar5[:, so + 14:so + 15]
                mraw = ar5[:, so + 16:so + 17]; nmv = ar5[:, so + 17:so + 18]
                xt = ar3[:, slot * 4096:(slot + 1) * 4096]
                P.waitp("sync", state["xt_rel"][slot])
                ldv = loaders[ti](slot)
                Wt("vector", "syd", ldv)
                P.waitp("vector", state["xl_rel"])
                P.op("vector", "reduce_sum", sums8,
                     xt.rearrange("p (a b) -> p a b", b=512), axis=AX)
                P.op("vector", "drain")
                P.op("vector", "reduce_sum", mraw, sums8, axis=AX)
                P.op("vector", "scalar_tensor_tensor", ar6[:, 0:4096], xt, 1.0, xt,
                     ALU.mult, ALU.mult, accum_out=ssq)
                P.op("vector", "drain")
                P.op("vector", "tensor_scalar_mul", mvec, mraw, 1.0 / D)
                P.op("vector", "tensor_scalar_mul", nmv, mraw, -1.0 / D)
                P.op("vector", "drain")
                P.op("vector", "tensor_scalar", nmsq, mvec, nmv, 1e-5,
                     ALU.mult, ALU.add)
                P.op("vector", "drain")
                P.op("vector", "scalar_tensor_tensor", varv, ssq, 1.0 / D, nmsq,
                     ALU.mult, ALU.add)
                P.op("vector", "drain")
                P.op("vector", "reciprocal", rcp, varv)
                P.op("vector", "drain", inc="v")
                vst = P.cnt["v"]
                Wt("scalar", "v", vst)
                P.op("scalar", "activation", rsv, rcp, AF.Sqrt)
                P.op("scalar", "drain")
                P.op("scalar", "mul", bbv, nmv, rsv)
                P.op("scalar", "drain")
                P.op("scalar", "activation", ar6[:, 0:4096], xt, AF.Identity,
                     bias=bbv, scale=rsv, inc="s")
                sap = P.cnt["s"]
                state["xt_rel"][slot] = ("s", sap)
                Wt("tensor", "s", sap)
                last_t = None
                for g in range(4):
                    pb = g % 2
                    P.waitp("tensor", state["pst_rel"][pb])
                    for k in range(8):
                        kk = g * 8 + k
                        P.op("tensor", "transpose",
                             pstr[:, pb * 1024 + k * 128:pb * 1024 + (k + 1) * 128],
                             ar6[:, kk * 128:(kk + 1) * 128], idn[:, :],
                             inc=("t" if k == 7 else None))
                    ttv = P.cnt["t"]
                    Wt("scalar", "t", ttv)
                    dst = ar1[:, :].rearrange("p (k s) -> p k s", k=KT)[:, g * 8:(g + 1) * 8,
                                                                       ti * 128:(ti + 1) * 128]
                    P.op("scalar", "copy", dst,
                         pstr[:, pb * 1024:pb * 1024 + 1024].rearrange("p (k s) -> p k s", s=128),
                         inc="s")
                    state["pst_rel"][pb] = ("s", P.cnt["s"])
                    last_t = ttv
                state["xl_rel"] = ("t", last_t)

        def wchunk_load(src_ap, slot, nelem):
            P.waitp("gpsimd", state["w_rel"][slot])
            return dma("gpsimd", ar2[:, slot * 8192:slot * 8192 + nelem], src_ap)

        def proj_tok(wdram, nch, csz, ntt, evac, extra=None, gate=None, lhs_gate=None,
                     stop_after=None):
            for ch in range(nch):
                slot = ch % 2
                wld = wchunk_load(wdram[ch].rearrange("p a c -> p (a c)"), slot, KT * csz)
                Wt("tensor", "gpd", wld)
                if gate is not None and ch == 0:
                    P.waitp("tensor", gate)
                if lhs_gate is not None and ch == 0:
                    Wt("tensor", "syd", lhs_gate)
                for tt in range(ntt):
                    bank = (ch * ntt + tt) % 4
                    P.waitp("tensor", state["ps_rel"][bank])
                    for k in range(KT):
                        last = (k == KT - 1 and extra is None)
                        P.op("tensor", "matmul", psb(bank)[:, 0:csz],
                             xlnT(k, tt * 128, 128),
                             ar2[:, slot * 8192 + k * csz:slot * 8192 + (k + 1) * csz],
                             start=(k == 0), stop=last,
                             inc=("t" if last else None))
                    if extra is not None:
                        extra(ch, tt, bank)
                    mmv = P.cnt["t"]
                    state["ps_rel"][bank] = evac(ch, tt, psb(bank)[:, 0:csz], mmv)
                    if stop_after == (ch, tt):
                        P.frozen = True
                state["w_rel"][slot] = ("t", P.cnt["t"])

        def proj_ch(wdram, nct, ntc, evac):
            for ct in range(nct):
                slot = ct % 2
                wld = wchunk_load(wdram[ct].rearrange("p a c -> p (a c)"), slot, KT * 128)
                Wt("tensor", "gpd", wld)
                for tc in range(ntc):
                    bank = (ct * ntc + tc) % 4
                    P.waitp("tensor", state["ps_rel"][bank])
                    for k in range(KT):
                        P.op("tensor", "matmul", psb(bank),
                             ar2[:, slot * 8192 + k * 128:slot * 8192 + (k + 1) * 128],
                             xlnT(k, tc * 512, 512),
                             start=(k == 0), stop=(k == KT - 1),
                             inc=("t" if k == KT - 1 else None))
                    mmv = P.cnt["t"]
                    state["ps_rel"][bank] = evac(ct, tc, psb(bank), mmv)
                state["w_rel"][slot] = ("t", P.cnt["t"])

        for half in (0, 1):
            tokoff = half * HALF
            P.barrier()
            loaders = [
                (lambda slot, ti=ti, to=tokoff:
                 dma("sync", ar3[:, slot * 4096:(slot + 1) * 4096],
                     d["x0"][to + ti * 128:to + (ti + 1) * 128, :]))
                for ti in range(NTH)]
            emit_ln(loaders, NTH)
            if upto == "ln" and half == 0:
                P.barrier()
                dma("sync", dbgT, ar1[:, :])
                dma("sync", dbgA5, ar5[:, 0:2048])
                dma("sync", dbgA6, ar6[:, :])
                P.frozen = True
            P.barrier()

            for mat in ("q", "k"):
                wA = d["wqa"] if mat == "q" else d["wka"]
                wB = d["wqb"] if mat == "q" else d["wkb"]
                dstT = qt if mat == "q" else kts

                def evacA(ch, tt, ps, mmv, dstT=dstT, half=half):
                    rslot = tt % 2
                    if upto == "qa0":
                        P.frozen = True
                        return ("s", P.cnt["s"])
                    rotst = ar5[:, 1712 + rslot * 128:1712 + rslot * 128 + 128]
                    stg = ar6[:, rslot * 512:rslot * 512 + 256]
                    P.waitp("scalar", state["rot_rel"][rslot])
                    Wt("scalar", "t", mmv)
                    ps3 = ps.rearrange("p (h c) -> p h c", c=128)
                    P.op("scalar", "copy",
                         rotst.rearrange("p (h c) -> p h c", c=64),
                         ps3[:, :, 0:64])
                    P.op("scalar", "copy",
                         stg.rearrange("p (h c) -> p h c", c=128)[:, :, 64:128],
                         ps3[:, :, 64:128], inc="s")
                    spv = P.cnt["s"]
                    if upto == "qaS":
                        P.frozen = True
                        return ("s", spv)
                    Wt("vector", "s", spv)
                    ti_glob = half * NTH + tt
                    cs = cosb[:, ti_glob * 32:(ti_glob + 1) * 32]
                    sn = sinb[:, ti_glob * 32:(ti_glob + 1) * 32]
                    scrA = ar5[:, 512:544]; scrB = ar5[:, 544:576]
                    scrC = ar5[:, 576:608]; scrD = ar5[:, 608:640]
                    for c2r in range(2):
                        bo = c2r * 128
                        e_in = rotst[:, c2r * 64:c2r * 64 + 32]
                        o_in = rotst[:, c2r * 64 + 32:c2r * 64 + 64]
                        P.op("vector", "tensor_tensor", scrA, e_in, cs, ALU.mult)
                        P.op("vector", "tensor_tensor", scrB, o_in, sn, ALU.mult)
                        P.op("vector", "tensor_tensor", scrC, o_in, cs, ALU.mult)
                        P.op("vector", "tensor_tensor", scrD, e_in, sn, ALU.mult)
                        P.op("vector", "drain")
                        P.op("vector", "tensor_tensor", stg[:, bo:bo + 32],
                             scrA, scrB, ALU.subtract)
                        P.op("vector", "tensor_tensor", stg[:, bo + 32:bo + 64],
                             scrC, scrD, ALU.add,
                             inc=("v" if c2r == 1 else None))
                    vv = P.cnt["v"]
                    if upto == "qaR":
                        P.frozen = True
                        return ("s", spv)
                    Wt("tensor", "v", vv)
                    Wt("tensor", "s", spv)
                    pb = rslot
                    P.waitp("tensor", state["pst_rel"][pb])
                    for c2 in range(2):
                        P.op("tensor", "transpose",
                             pstr[:, pb * 1024 + c2 * 128:pb * 1024 + (c2 + 1) * 128],
                             stg[:, c2 * 128:(c2 + 1) * 128], idn[:, :],
                             inc=("t" if c2 == 1 else None))
                    trv = P.cnt["t"]
                    state["rot_rel"][rslot] = ("t", trv)
                    if upto == "qaT":
                        P.frozen = True
                        return ("s", spv)
                    qslot = tt % 2
                    P.waitp("scalar", state["stg_rel"][qslot])
                    Wt("scalar", "t", trv)
                    qstg = ar6[:, 2048 + qslot * 512:2048 + qslot * 512 + 256]
                    P.op("scalar", "copy", qstg, pstr[:, pb * 1024:pb * 1024 + 256], inc="s")
                    sv = P.cnt["s"]
                    state["pst_rel"][pb] = ("s", sv)
                    Wt("sync", "s", sv)
                    lastd = None
                    for c2 in range(2):
                        hloc = ch * 2 + c2
                        lastd = dma("sync",
                                    dstT[2 * hloc, :, tokoff + tt * 128:tokoff + (tt + 1) * 128],
                                    qstg[:, c2 * 128:(c2 + 1) * 128])
                    state["stg_rel"][qslot] = ("syd", lastd)
                    return ("s", spv)
                sa = None
                if half == 0 and mat == "q" and upto in ("qa1", "qa2", "qa3"):
                    sa = {"qa1": (0, 0), "qa2": (0, 7), "qa3": (1, 7)}[upto]
                proj_tok(wA, 4, 256, NTH, evacA, stop_after=sa)
                if upto == "qa" and half == 0 and mat == "q":
                    P.frozen = True

                def evacB(ct, tc, ps, mmv, dstT=dstT):
                    qslot = tc % 2
                    P.waitp("scalar", state["stg_rel"][qslot])
                    Wt("scalar", "t", mmv)
                    stg = ar6[:, 2048 + qslot * 512:2048 + qslot * 512 + 512]
                    P.op("scalar", "copy", stg, ps, inc="s")
                    sv = P.cnt["s"]
                    Wt("sync", "s", sv)
                    lastd = dma("sync",
                                dstT[2 * ct + 1, :, tokoff + tc * 512:tokoff + tc * 512 + 512],
                                stg)
                    state["stg_rel"][qslot] = ("syd", lastd)
                    return ("s", sv)
                proj_ch(wB, NHC, 2, evacB)
                if upto == "qb" and half == 0 and mat == "q":
                    P.frozen = True

            def evacV(ch, tt, ps, mmv):
                qslot = tt % 2
                P.waitp("scalar", state["stg_rel"][qslot])
                Wt("scalar", "t", mmv)
                stg = ar6[:, 2048 + qslot * 512:2048 + qslot * 512 + 256]
                P.op("scalar", "copy", stg, ps, inc="s")
                sv = P.cnt["s"]
                Wt("sync", "s", sv)
                lastd = dma("sync", vs[tokoff + tt * 128:tokoff + (tt + 1) * 128,
                                       ch * 256:(ch + 1) * 256], stg)
                state["stg_rel"][qslot] = ("syd", lastd)
                return ("s", sv)
            proj_tok(d["wv"], 8, 256, NTH, evacV)
            if upto == "v0" and half == 0:
                P.frozen = True

        P.barrier()
        if upto == "qkv":
            P.frozen = True

        qh = ar1[:, 0:4096]; kh = ar1[:, 4096:8192]; vh = ar1[:, 8192:12288]
        aoh = ar1[:, 12288:16384]
        pbuf = ar1[:, 16384:18432]
        ptb = ar1[:, 18432:26624]

        def attention(h, causal, expscale, dst_mode):
            P.waitp("sync", state["qkv_rel"])
            rbig_ld = None
            if dst_mode == "b":
                P.waitp("sync", state["rbig_rel"])
                rbig_ld = dma_dyn("sync", lambda E, rt, h=h: E.dma_start(
                    out=ar3[:, 4096:8192].rearrange("p (a c) -> p a c", c=256),
                    in_=cc2o.ap().rearrange("(a p) c -> p a c", p=128)
                    [:, :, bass.ds(rt["v"] * 2048 + h * 256, 256)]))
            l1 = dma("sync", qh[:, :].rearrange("p (a s) -> p a s", a=2),
                     qt[2 * h:2 * h + 2].transpose([1, 0, 2]))
            l2 = dma("sync", kh[:, :].rearrange("p (a s) -> p a s", a=2),
                     kts[2 * h:2 * h + 2].transpose([1, 0, 2]))
            l3 = dma("sync", vh[:, :].rearrange("p (a s) -> p a s", a=16),
                     vs.rearrange("(a p) c -> p a c", p=128)[:, :, h * 256:(h + 1) * 256])
            lastd = None
            for qi in range(NTS):
                aslot = qi % 2
                ao = 1024 + aslot * 24
                mxc = ar5[:, ao:ao + 6]; mxv = ar5[:, ao + 6:ao + 8]
                smc = ar5[:, ao + 8:ao + 14]; smv = ar5[:, ao + 14:ao + 15]
                rpv = ar5[:, ao + 15:ao + 16]
                lgst = ar3[:, aslot * 2048:(aslot + 1) * 2048]
                nk = (qi + 1) * 128 if causal else SEQ
                nkc = (nk + 511) // 512
                nfull, rem = nk // 512, nk % 512
                nseg = nfull + (1 if rem else 0)
                Wt("tensor", "syd", max(l1, l2))
                for kc in range(nkc):
                    nkk = min(512, nk - kc * 512)
                    P.waitp("tensor", state["ps_rel"][kc])
                    for ct in range(2):
                        P.op("tensor", "matmul", psb(kc)[:, 0:nkk],
                             qh[:, ct * 2048 + qi * 128:ct * 2048 + (qi + 1) * 128],
                             kh[:, ct * 2048 + kc * 512:ct * 2048 + kc * 512 + nkk],
                             start=(ct == 0), stop=(ct == 1),
                             inc=("t" if ct == 1 else None))
                mmv = P.cnt["t"]
                Wt("scalar", "t", mmv)
                P.waitp("scalar", state["lg_rel"][aslot])
                for kc in range(nkc):
                    nkk = min(512, nk - kc * 512)
                    P.op("scalar", "copy", lgst[:, kc * 512:kc * 512 + nkk],
                         psb(kc)[:, 0:nkk], inc=("s" if kc == nkc - 1 else None))
                scp = P.cnt["s"]
                for kc in range(nkc):
                    state["ps_rel"][kc] = ("s", scp)
                Wt("vector", "s", scp)
                if nfull:
                    P.op("vector", "reduce_max", mxc[:, 0:nfull],
                         lgst[:, 0:nfull * 512].rearrange("p (a b) -> p a b", b=512),
                         axis=AX)
                if rem:
                    P.op("vector", "reduce_max", mxc[:, nfull:nfull + 1],
                         lgst[:, nfull * 512:nk], axis=AX)
                P.op("vector", "drain")
                P.op("vector", "reduce_max", mxv[:, 0:1], mxc[:, 0:nseg], axis=AX)
                P.op("vector", "drain")
                P.op("vector", "tensor_scalar_mul", mxv[:, 1:2], mxv[:, 0:1], -expscale, inc="v")
                vmx = P.cnt["v"]
                state["lg_rel"][aslot] = ("v", vmx)
                Wt("scalar", "v", vmx)
                P.waitp("scalar", state["pbuf_rel"])
                P.op("scalar", "activation", pbuf[:, 0:nk], lgst[:, 0:nk], AF.Exp,
                     bias=mxv[:, 1:2], scale=expscale, inc="s")
                sex = P.cnt["s"]
                Wt("vector", "s", sex)
                if causal:
                    P.op("vector", "tensor_tensor", pbuf[:, qi * 128:(qi + 1) * 128],
                         pbuf[:, qi * 128:(qi + 1) * 128], ltm[:, :], ALU.mult)
                    P.op("vector", "drain")
                if nfull:
                    P.op("vector", "reduce_sum", smc[:, 0:nfull],
                         pbuf[:, 0:nfull * 512].rearrange("p (a b) -> p a b", b=512),
                         axis=AX)
                if rem:
                    P.op("vector", "reduce_sum", smc[:, nfull:nfull + 1],
                         pbuf[:, nfull * 512:nk], axis=AX)
                P.op("vector", "drain")
                P.op("vector", "reduce_sum", smv, smc[:, 0:nseg], axis=AX)
                P.op("vector", "drain")
                P.op("vector", "reciprocal", rpv, smv)
                P.op("vector", "drain")
                P.op("vector", "tensor_scalar_mul", pbuf[:, 0:nk], pbuf[:, 0:nk], rpv, inc="v")
                vpn = P.cnt["v"]
                Wt("tensor", "v", vpn)
                nkt = (qi + 1) if causal else NTS
                last_tr = None
                for g in range((nkt + 7) // 8):
                    pb = g % 2
                    P.waitp("tensor", state["pst_rel"][pb])
                    if g == 0:
                        P.waitp("tensor", state["ptb_rel"])
                    gend = min(8, nkt - g * 8)
                    for k in range(gend):
                        kk = g * 8 + k
                        P.op("tensor", "transpose",
                             pstr[:, pb * 1024 + k * 128:pb * 1024 + (k + 1) * 128],
                             pbuf[:, kk * 128:(kk + 1) * 128], idn[:, :],
                             inc=("t" if k == gend - 1 else None))
                    trv = P.cnt["t"]
                    last_tr = trv
                    Wt("scalar", "t", trv)
                    dst = ptb[:, :].rearrange("p (k s) -> p k s", s=512)[:, g * 8:g * 8 + gend,
                                                                        (qi % 4) * 128:(qi % 4) * 128 + 128]
                    P.op("scalar", "copy", dst,
                         pstr[:, pb * 1024:pb * 1024 + gend * 128]
                         .rearrange("p (k s) -> p k s", s=128), inc="s")
                    state["pst_rel"][pb] = ("s", P.cnt["s"])
                state["pbuf_rel"] = ("t", last_tr)
                sptv = P.cnt["s"]
                if dst_mode == "b":
                    bank = 4 + (qi % 2)
                    Wt("tensor", "s", sptv)
                    Wt("tensor", "syd", l3)
                    P.waitp("tensor", state["ps_rel"][bank])
                    for kk in range(NTS):
                        P.op("tensor", "matmul", psb(bank)[:, 0:256],
                             ptb[:, :].rearrange("p (k s) -> p k s", s=512)
                             [:, kk, (qi % 4) * 128:(qi % 4) * 128 + 128],
                             vh[:, kk * 256:(kk + 1) * 256],
                             start=(kk == 0), stop=(kk == NTS - 1),
                             inc=("t" if kk == NTS - 1 else None))
                    avv = P.cnt["t"]
                    state["ptb_rel"] = ("t", avv)
                    rslot = qi % 2
                    resid = ar3[:, 4096 + qi * 256:4096 + (qi + 1) * 256]
                    ost = ar5[:, 1200 + rslot * 256:1200 + (rslot + 1) * 256]
                    Wt("scalar", "t", avv)
                    P.waitp("scalar", state["ost_rel"][rslot])
                    P.op("scalar", "copy", ost, psb(bank)[:, 0:256], inc="s")
                    sov = P.cnt["s"]
                    state["ps_rel"][bank] = ("s", sov)
                    Wt("vector", "s", sov)
                    Wt("vector", "syd", rbig_ld)
                    P.op("vector", "tensor_tensor", ost, ost, resid, ALU.add, inc="v")
                    vev = P.cnt["v"]
                    state["rbig_rel"] = ("v", vev)
                    Wt("sync", "v", vev)
                    lastd = dma("sync", cc3i.ap()[qi * 128:(qi + 1) * 128,
                                                  h * 256:(h + 1) * 256], ost)
                    state["ost_rel"][rslot] = ("syd", lastd)
                elif qi % 4 == 3:
                    qc = qi // 4
                    Wt("tensor", "s", sptv)
                    Wt("tensor", "syd", l3)
                    av_last = None
                    for c2 in range(2):
                        bank = 4 + c2
                        P.waitp("tensor", state["ps_rel"][bank])
                        for qq in range(4):
                            qtl = qc * 4 + qq
                            nkt2 = (qtl + 1) if causal else NTS
                            for kk in range(nkt2):
                                P.op("tensor", "matmul",
                                     psb(bank)[:, qq * 128:(qq + 1) * 128],
                                     vh[:, kk * 256 + c2 * 128:kk * 256 + (c2 + 1) * 128],
                                     ptb[:, :].rearrange("p (k s) -> p k s", s=512)
                                     [:, kk, qq * 128:(qq + 1) * 128],
                                     start=(kk == 0), stop=(kk == nkt2 - 1),
                                     inc=("t" if (kk == nkt2 - 1 and qq == 3) else None))
                        avv = P.cnt["t"]
                        av_last = avv
                        Wt("scalar", "t", avv)
                        P.waitp("scalar", state["aoh_rel"])
                        P.op("scalar", "copy",
                             aoh[:, c2 * 2048 + qc * 512:c2 * 2048 + (qc + 1) * 512],
                             psb(bank)[:, 0:512], inc="s")
                        state["ps_rel"][bank] = ("s", P.cnt["s"])
                    state["ptb_rel"] = ("t", av_last)
            state["qkv_rel"] = ("t", P.cnt["t"])
            if dst_mode == "a":
                sfin = P.cnt["s"]
                Wt("sync", "s", sfin)
                dma("sync", cc1i.ap()[h * 256:h * 256 + 128, :], aoh[:, 0:2048])
                lastd = dma("sync", cc1i.ap()[h * 256 + 128:h * 256 + 256, :], aoh[:, 2048:4096])
                state["aoh_rel"] = ("syd", lastd)
            return lastd

        last_w = None
        for h in range(NHC):
            last_w = attention(h, causal=True, expscale=1.0 / 16.0, dst_mode="a")
        if upto == "ao":
            P.frozen = True
        Wt("gpsimd", "syd", last_w)
        P.op("gpsimd", "collective_compute", "AllGather", ALU.bypass,
             replica_groups=RG, ins=[cc1i.ap().opt()], outs=[cc1o.ap().opt()], inc="cc")
        cc1v = P.cnt["cc"]
        P.barrier()

        Wt("sync", "cc", cc1v)
        lao = dma_dyn("sync", lambda E, rt:
                      E.dma_start(out=ar1[:, :].rearrange("p (k s) -> p k s", k=KT),
                                  in_=cc1o.ap().rearrange("(k p) s -> p k s", p=128)
                                  [:, :, bass.ds(rt["v"] * 1024, 1024)]))

        def evacO(ch, tt, ps, mmv):
            rslot = (ch * NTH + tt) % 2
            P.waitp("sync", state["res_rel"][rslot])
            resid = ar3[:, rslot * 256:(rslot + 1) * 256]
            rldv = dma("sync", resid, d["x0h"][tt * 128:(tt + 1) * 128,
                                               ch * 256:(ch + 1) * 256])
            ost = ar5[:, 1200 + rslot * 256:1200 + (rslot + 1) * 256]
            Wt("scalar", "t", mmv)
            P.waitp("scalar", state["ost_rel"][rslot])
            P.op("scalar", "copy", ost, ps, inc="s")
            sov = P.cnt["s"]
            Wt("vector", "s", sov)
            Wt("vector", "syd", rldv)
            P.op("vector", "tensor_tensor", ost, ost, resid, ALU.add, inc="v")
            vev = P.cnt["v"]
            state["res_rel"][rslot] = ("v", vev)
            Wt("sync", "v", vev)
            odv = dma("sync", x1s[tt * 128:(tt + 1) * 128, ch * 256:(ch + 1) * 256], ost)
            state["ost_rel"][rslot] = ("syd", odv)
            return ("s", sov)

        proj_tok(d["wo"], 16, 256, NTH, evacO, lhs_gate=lao)
        P.barrier()

        def copy_out(src_ap_fn):
            for tt in range(NTH):
                slot = tt % 2
                P.waitp("sync", state["xt_rel"][slot])
                xt = ar3[:, slot * 4096:(slot + 1) * 4096]
                lv = src_ap_fn(tt, xt)
                if lv is not None:
                    Wt("sync", "syd", lv)
                v2 = dma("sync", out[tt * 128:(tt + 1) * 128, :], xt)
                state["xt_rel"][slot] = ("syd", v2)

        if upto == "x1":
            copy_out(lambda tt, xt: dma("sync", xt, x1s[tt * 128:(tt + 1) * 128, :]))
        else:
            def ffn_phase(ln_loaders, resid_src_fn, dst_fn):
                emit_ln(ln_loaders, NTH)
                P.barrier()
                for tt in range(NTH):
                    bank = 4
                    P.waitp("tensor", state["ps_rel"][bank])
                    for k in range(KT):
                        P.op("tensor", "matmul", psb(bank)[:, 0:48],
                             xlnT(k, tt * 128, 128), lAb[:, k * 48:(k + 1) * 48],
                             start=(k == 0), stop=(k == KT - 1),
                             inc=("t" if k == KT - 1 else None))
                    mmv = P.cnt["t"]
                    tslot = tt % 2
                    tlst = ar5[:, 1968:2016]
                    Wt("scalar", "t", mmv)
                    P.waitp("scalar", state["tl_rel"])
                    P.op("scalar", "copy", tlst, psb(bank)[:, 0:48], inc="s")
                    stv = P.cnt["s"]
                    state["ps_rel"][bank] = ("s", stv)
                    Wt("vector", "s", stv)
                    P.waitp("vector", state["rot_rel"][tslot])
                    P.op("vector", "tensor_tensor", tlora[:, tslot * 48:(tslot + 1) * 48],
                         tlst, maskw[:, :], ALU.mult, inc="v")
                    vvm = P.cnt["v"]
                    state["tl_rel"] = ("v", vvm)
                    Wt("tensor", "v", vvm)
                    pb = tslot
                    P.waitp("tensor", state["pst_rel"][pb])
                    P.op("tensor", "transpose", pstr[0:48, pb * 1024:pb * 1024 + 128],
                         tlora[:, tslot * 48:(tslot + 1) * 48], idn[:, :], inc="t")
                    trv = P.cnt["t"]
                    state["rot_rel"][tslot] = ("t", trv)
                    Wt("scalar", "t", trv)
                    P.op("scalar", "copy", tloraT[:, tt * 128:(tt + 1) * 128],
                         pstr[0:48, pb * 1024:pb * 1024 + 128], inc="s")
                    state["pst_rel"][pb] = ("s", P.cnt["s"])
                sl_done = P.cnt["s"]

                def extraF(ch, tt, bank):
                    P.op("tensor", "matmul", psb(bank)[:, 0:256],
                         onesb[0:1, 0:128], brow[0:1, ch * 256:(ch + 1) * 256],
                         start=False, stop=False)
                    P.op("tensor", "matmul", psb(bank)[:, 0:256],
                         tloraT[:, tt * 128:(tt + 1) * 128],
                         lBb[:, ch * 256:(ch + 1) * 256],
                         start=False, stop=True, inc="t")

                def evacF(ch, tt, ps, mmv):
                    rslot = (ch * NTH + tt) % 2
                    P.waitp("sync", state["res_rel"][rslot])
                    resid = ar3[:, rslot * 256:(rslot + 1) * 256]
                    rldv = resid_src_fn(ch, tt, resid)
                    ost = ar5[:, 1200 + rslot * 256:1200 + (rslot + 1) * 256]
                    Wt("scalar", "t", mmv)
                    P.waitp("scalar", state["ost_rel"][rslot])
                    P.op("scalar", "copy", ost, ps, inc="s")
                    sov = P.cnt["s"]
                    Wt("vector", "s", sov)
                    Wt("vector", "syd", rldv)
                    P.op("vector", "tensor_tensor", ost, ost, resid, ALU.add, inc="v")
                    vev = P.cnt["v"]
                    state["res_rel"][rslot] = ("v", vev)
                    Wt("sync", "v", vev)
                    odv = dst_fn(ch, tt, ost)
                    state["ost_rel"][rslot] = ("syd", odv)
                    return ("s", sov)

                proj_tok(d["wf"], 16, 256, NTH, evacF, extra=extraF, gate=("s", sl_done))

            f1_loaders = [
                (lambda slot, ti=ti:
                 dma("sync", ar3[:, slot * 4096:(slot + 1) * 4096],
                     x1s[ti * 128:(ti + 1) * 128, :]))
                for ti in range(NTH)]
            ffn_phase(f1_loaders,
                      lambda ch, tt, resid: dma("sync", resid,
                                                x1s[tt * 128:(tt + 1) * 128,
                                                    ch * 256:(ch + 1) * 256]),
                      lambda ch, tt, ost: dma("sync", cc2i.ap()[tt * 128:(tt + 1) * 128,
                                                                ch * 256:(ch + 1) * 256], ost))
            P.wait("gpsimd", "syd", P.cnt["syd"])
            P.op("gpsimd", "collective_compute", "AllGather", ALU.bypass,
                 replica_groups=RG, ins=[cc2i.ap().opt()], outs=[cc2o.ap().opt()], inc="cc")
            cc2v = P.cnt["cc"]
            P.barrier()

            if upto == "x2":
                copy_out(lambda tt, xt: dma("sync", xt, cc2i.ap()[tt * 128:(tt + 1) * 128, :]))
            else:
                for half in (0, 1):
                    tokoff = half * HALF
                    P.barrier()
                    def mk_loader(ti, to=tokoff):
                        def L(slot, ti=ti, to=to):
                            Wt("sync", "cc", cc2v)
                            return dma("sync", ar3[:, slot * 4096:(slot + 1) * 4096],
                                       cc2o.ap()[to + ti * 128:to + (ti + 1) * 128, :])
                        return L
                    emit_ln([mk_loader(ti) for ti in range(NTH)], NTH)
                    P.barrier()

                    def evacQ(ct, tc, ps, mmv):
                        qslot = tc % 2
                        P.waitp("scalar", state["stg_rel"][qslot])
                        Wt("scalar", "t", mmv)
                        stg = ar6[:, qslot * 512:qslot * 512 + 512]
                        P.op("scalar", "activation", stg, ps, AF.Identity,
                             bias=bqcs[:, ct:ct + 1], scale=1.0, inc="s")
                        sv = P.cnt["s"]
                        bank = 4 + (tc % 2)
                        Wt("tensor", "s", sv)
                        P.waitp("tensor", state["ps_rel"][bank])
                        P.op("tensor", "matmul", psb(bank),
                             tblk[:, (ct // 2) * 128:(ct // 2 + 1) * 128], stg,
                             start=True, stop=True, inc="t")
                        tv2 = P.cnt["t"]
                        state["stg_rel"][qslot] = ("t", tv2)
                        Wt("scalar", "t", tv2)
                        P.waitp("scalar", state["rot_rel"][qslot])
                        stg2 = ar6[:, 2048 + qslot * 512:2048 + qslot * 512 + 512]
                        P.op("scalar", "copy", stg2, psb(bank), inc="s")
                        sv2 = P.cnt["s"]
                        state["ps_rel"][bank] = ("s", sv2)
                        Wt("sync", "s", sv2)
                        odv = dma("sync", qt[ct, :, tokoff + tc * 512:tokoff + tc * 512 + 512],
                                  stg2)
                        state["rot_rel"][qslot] = ("syd", odv)
                        return ("s", sv)
                    proj_ch(d["wbq"], 16, 2, evacQ)

                    def evacK(ct, tc, ps, mmv):
                        qslot = tc % 2
                        P.waitp("scalar", state["stg_rel"][qslot])
                        Wt("scalar", "t", mmv)
                        stg = ar6[:, qslot * 512:qslot * 512 + 512]
                        P.op("scalar", "activation", stg, ps, AF.Identity,
                             bias=bkcs[:, ct:ct + 1], scale=1.0, inc="s")
                        sv = P.cnt["s"]
                        Wt("sync", "s", sv)
                        odv = dma("sync", kts[ct, :, tokoff + tc * 512:tokoff + tc * 512 + 512],
                                  stg)
                        state["stg_rel"][qslot] = ("syd", odv)
                        return ("s", sv)
                    proj_ch(d["wbk"], 16, 2, evacK)

                    def extraVb(ch, tt, bank):
                        P.op("tensor", "matmul", psb(bank)[:, 0:256],
                             onesb[0:1, 0:128], bvr[0:1, ch * 256:(ch + 1) * 256],
                             start=False, stop=True, inc="t")

                    def evacVb(ch, tt, ps, mmv):
                        qslot = tt % 2
                        P.waitp("scalar", state["stg_rel"][qslot])
                        Wt("scalar", "t", mmv)
                        stg = ar6[:, qslot * 512:qslot * 512 + 256]
                        P.op("scalar", "copy", stg, ps, inc="s")
                        sv = P.cnt["s"]
                        Wt("sync", "s", sv)
                        odv = dma("sync", vs[tokoff + tt * 128:tokoff + (tt + 1) * 128,
                                             ch * 256:(ch + 1) * 256], stg)
                        state["stg_rel"][qslot] = ("syd", odv)
                        return ("s", sv)
                    proj_tok(d["wbv"], 8, 256, NTH, evacVb, extra=extraVb)

                P.barrier()
                if upto == "bqkv":
                    P.frozen = True
                last_w = None
                for h in range(NHC):
                    last_w = attention(h, causal=False, expscale=1.0, dst_mode="b")
                if upto == "bo":
                    P.frozen = True
                P.wait("gpsimd", "syd", P.cnt["syd"])
                P.op("gpsimd", "collective_compute", "AllGather", ALU.bypass,
                     replica_groups=RG, ins=[cc3i.ap().opt()], outs=[cc3o.ap().opt()],
                     inc="cc")
                cc3v = P.cnt["cc"]
                Wt("sync", "cc", cc3v)
                dma_dyn("sync", lambda E, rt: E.dma_start(
                    out=x3a, in_=cc3o.ap()[bass.ds(rt["v"] * 1024, 1024), :]))
                dma_dyn("sync", lambda E, rt: E.dma_start(
                    out=x3b, in_=cc3o.ap()[bass.ds(rt["v"] * 1024 + 2048, 1024), :]))
                P.barrier()

                if upto == "x3":
                    def x3src(tt, xt):
                        dma("sync", xt[:, 0:2048], x3a[tt * 128:(tt + 1) * 128, :])
                        return dma("sync", xt[:, 2048:4096],
                                   x3b[tt * 128:(tt + 1) * 128, :])
                    copy_out(x3src)
                else:
                    def mk_f2_loader(ti):
                        def L(slot, ti=ti):
                            xt = ar3[:, slot * 4096:(slot + 1) * 4096]
                            dma("sync", xt[:, 0:2048], x3a[ti * 128:(ti + 1) * 128, :])
                            return dma("sync", xt[:, 2048:4096],
                                       x3b[ti * 128:(ti + 1) * 128, :])
                        return L

                    def f2_resid(ch, tt, resid):
                        if ch < 8:
                            return dma("sync", resid,
                                       x3a[tt * 128:(tt + 1) * 128, ch * 256:(ch + 1) * 256])
                        return dma("sync", resid,
                                   x3b[tt * 128:(tt + 1) * 128, (ch - 8) * 256:(ch - 7) * 256])

                    ffn_phase([mk_f2_loader(ti) for ti in range(NTH)],
                              f2_resid,
                              lambda ch, tt, ost: dma("sync",
                                                      out[tt * 128:(tt + 1) * 128,
                                                          ch * 256:(ch + 1) * 256], ost))

        P.frozen = False
        P.wait("sync", "syd", P.cnt["syd"])
        P.wait("gpsimd", "gpd", P.cnt["gpd"])
        P.barrier()

        def mk(e):
            def f(E):
                for fn in P.ops[e]:
                    fn(E)
            return f
        block.tensor(mk("tensor"))
        block.vector(mk("vector"))
        block.scalar(mk("scalar"))
        block.sync(mk("sync"))
        block.gpsimd(mk("gpsimd"))
    return nc



def _prep_inputs(inputs):
    f32 = lambda a: np.ascontiguousarray(np.asarray(a, dtype=np.float32))
    X = f32(inputs["L0_attn_a_hidden_states"])
    wq = f32(inputs["L0_attn_a_wq"]); wk = f32(inputs["L0_attn_a_wk"])
    wv = f32(inputs["L0_attn_a_wv"]); wo = f32(inputs["L0_attn_a_wo"])
    Wbq = f32(inputs["L0_attn_b_Wq"]); Wbk = f32(inputs["L0_attn_b_Wk"])
    Wbv = f32(inputs["L0_attn_b_Wv"])
    bq = f32(inputs["L0_attn_b_bq"]); bk = f32(inputs["L0_attn_b_bk"])
    bv = f32(inputs["L0_attn_b_bv"])
    bwt = f32(inputs["L0_attn_b_bias_weight"])
    wf = f32(inputs["L0_ffn_weight"]); fb = f32(inputs["L0_ffn_bias"])
    lA = f32(inputs["L0_ffn_lora_A"]); lB = f32(inputs["L0_ffn_lora_B"])
    lsi = np.ascontiguousarray(np.asarray(inputs["L0_ffn_lora_start_indices"])
                               .astype(np.int32).reshape(1, 4))

    cosc, sinc = _sincos()
    bf = lambda a: np.ascontiguousarray(a.astype(ml_dtypes.bfloat16))
    consts = {
        "identc": bf(np.eye(128, dtype=np.float32)),
        "ltmask": bf(np.tril(np.ones((128, 128), np.float32))),
        "cosc": cosc, "sinc": sinc,
        "tbasis": _t_basis(),
        "onesc": bf(np.ones((1, 512), np.float32)),
        "i3": np.arange(3, dtype=np.float32).reshape(1, 3),
        "fbias": fb.reshape(1, D),
        "lsi": lsi,
        "lA": np.ascontiguousarray(
            lA.transpose(2, 0, 1).reshape(D, 48)
            .reshape(KT, 128, 48).transpose(1, 0, 2)[None]),
        "lB": np.ascontiguousarray(lB.transpose(0, 2, 1).reshape(48, D)),
        "wo": _rhs_tiles(wo, range(D), 256),
        "wf": _rhs_tiles(wf, range(D), 256),
    }
    maps = []
    for c in range(8):
        b, hg = c // 2, c % 2
        heads = list(range(hg * 8, hg * 8 + 8))
        coutsA = sum((_perm_tileA(h) for h in heads), [])
        coutsB = sum(([h * HD + i for i in range(128, 256)] for h in heads), [])
        coutsV = sum(([h * HD + i for i in range(256)] for h in heads), [])
        coutsHB = list(range(hg * 2048, (hg + 1) * 2048))
        m = dict(consts)
        m["x0"] = X[b]
        m["x0h"] = np.ascontiguousarray(X[b, hg * HALF:(hg + 1) * HALF])
        m["wqa"] = _rhs_tiles(wq, coutsA, 256)
        m["wka"] = _rhs_tiles(wk, coutsA, 256)
        m["wqb"] = _lhsT_tiles(wq, coutsB)
        m["wkb"] = _lhsT_tiles(wk, coutsB)
        m["wv"] = _rhs_tiles(wv, coutsV, 256)
        m["wbq"] = _lhsT_tiles(Wbq, coutsHB)
        m["wbk"] = _lhsT_tiles(Wbk, coutsHB)
        m["wbv"] = _rhs_tiles(Wbv, coutsHB, 256)
        m["bqc"] = np.ascontiguousarray(bq[coutsHB].reshape(16, 128).T)
        m["bkc"] = np.ascontiguousarray(bk[coutsHB].reshape(16, 128).T)
        m["bvrow"] = np.ascontiguousarray(bv[coutsHB].reshape(1, 2048))
        m["bw"] = np.ascontiguousarray(bwt[heads].reshape(1, 3 * NHC))
        m["myb"] = np.array([[float(b)]], dtype=np.float32)
        m["offs"] = np.array([[hg * 8, hg, hg * 8, 0]], dtype=np.int32)
        maps.append(m)
    return maps


def _impl(inputs, upto):
    maps = _prep_inputs(inputs)
    nc = build(upto)
    mybir.codegen_inst_isa_subclasses(nc)
    res = run_bass_kernel_spmd(nc, maps, core_ids=list(range(8)))
    if upto in ("ln", "qkv", "qa", "qa0", "qaS", "qaR", "qaT", "qa1", "qa2", "qa3", "qb", "v0", "ao", "bqkv", "bo"):
        return res.results
    outf = np.empty((B, SEQ, D), np.float32)
    for c in range(8):
        b, hg = c // 2, c % 2
        outf[b, hg * HALF:(hg + 1) * HALF] = res.results[c]["out"]
    return outf


def kernel(**inputs):
    return _impl(inputs, "full")
